# version 10 (final)
# speedup vs baseline: 1.2778x; 1.0022x over previous
import numpy as np
import ml_dtypes
import concourse.bass as bass
import concourse.mybir as mybir
from concourse.bass_utils import run_bass_kernel_spmd

F32 = mybir.dt.float32
BF16 = mybir.dt.bfloat16
F8 = mybir.dt.float8e4
I32 = mybir.dt.int32
ALU = mybir.AluOpType
ACT = mybir.ActivationFunctionType
AX = mybir.AxisListType

B, L, D = 4, 4096, 1024
DI, DS, DTR, DC = 2048, 16, 64, 4
TC = 512
NCH = L // TC
NDT = DI // 128
NKD = D // 128
HEADS, DH = 16, 64
NTT = L // 128
OWN = NTT // 2
NB = 4


def _hadamard(n):
    h = np.array([[1.0]], dtype=np.float64)
    while h.shape[0] < n:
        h = np.block([[h, h], [h, -h]])
    return (h / np.sqrt(n)).astype(np.float32)


def build(stage="full"):
    nc = bass.Bass("TRN2", target_bir_lowering=False,
                   detect_race_conditions=False)

    def din(name, shape, dt=F32):
        return nc.dram_tensor(name, list(shape), dt, kind="ExternalInput").ap()

    def dint(name, shape, dt=F32):
        return nc.dram_tensor(name, list(shape), dt).ap()

    x_l0 = din("x_l0", (L, D))
    x_nat = din("x_nat", (L, D))
    in_wT = din("in_wT", (D, 2 * DI))
    conv_wc = din("conv_wc", (128, NDT * DC))
    conv_bc = din("conv_bc", (128, NDT))
    xproj_wT = din("xproj_wT", (DI, 96))
    dt_wT = din("dt_wT", (DTR, DI))
    dt_bc = din("dt_bc", (128, NDT))
    A_logc = din("A_logc", (128, NDT * DS))
    Dvecc = din("Dvecc", (128, NDT))
    mout_wT = din("mout_wT", (DI, D))
    outw_halfT = din("outw_halfT", (D, D))
    out_bv = din("out_bv", (1, D))
    gammav = din("gammav", (1, D))
    betav = din("betav", (1, D))
    ffn_w = din("ffn_w", (D, D))
    hadm = din("hadm", (D, D), BF16)
    WqT = din("WqT", (DH, DH))
    WkT = din("WkT", (DH, DH))
    WvT = din("WvT", (DH, DH))
    WoT = din("WoT", (D, D))
    l1k = din("l1k", (D, D), BF16)
    s_inv = din("s_inv", (1, 1))
    s_kv = din("s_kv", (1, 1))
    l1biasv = din("l1biasv", (1, D), BF16)
    eye = din("eye128", (128, 128))
    eyeJ = din("eyeJ", (128, 128))
    maskd = din("maskd", (OWN * 128, 256))
    is_bwd = din("is_bwd", (1, 1), I32)

    out_own = nc.dram_tensor("out_own", [OWN * 128, D], F32, kind="ExternalOutput").ap()
    dbg = None
    if stage in ("x1", "x2"):
        dbg = nc.dram_tensor("dbg", [L, D], F32, kind="ExternalOutput").ap()
    elif stage == "xc":
        dbg = nc.dram_tensor("dbg", [NCH * NDT * 128, 2 * TC], F32, kind="ExternalOutput").ap()
    elif stage == "pag":
        dbg = nc.dram_tensor("dbg", [L, D], F32, kind="ExternalOutput").ap()
    elif stage == "l0dbg":
        dbg = nc.dram_tensor("dbg", [3 * 128, NDT * TC], F32, kind="ExternalOutput").ap()
    elif stage == "x3":
        dbg = nc.dram_tensor("dbg", [OWN * 128, D], F32, kind="ExternalOutput").ap()
        dbg2 = nc.dram_tensor("dbg2", [OWN * 128, D], F32, kind="ExternalOutput").ap()
        dbg3 = nc.dram_tensor("dbg3", [128, 2048], F32, kind="ExternalOutput").ap()

    inw_hi = dint("inw_hi", (D, 2 * DI), BF16)
    inw_lo = dint("inw_lo", (D, 2 * DI), BF16)
    mout_hi = dint("mout_hi", (DI, D), BF16)
    mout_lo = dint("mout_lo", (DI, D), BF16)
    outw_hi = dint("outw_hi", (D, D), BF16)
    outw_lo = dint("outw_lo", (D, D), BF16)
    wo_hi = dint("wo_hi", (D, D), BF16)
    wo_lo = dint("wo_lo", (D, D), BF16)
    ternT_d = dint("ternT_d", (D, D), BF16)
    kq_d = dint("kq_d", (D, D), F8)
    zs_dram = dint("zs_dram", (NCH * NDT * 128, TC))
    xc_dram = dint("xc_dram", (NCH * NDT * 128, TC))
    p_ag_in = dint("p_ag_in", (L, D))
    p_ag_out = dint("p_ag_out", (2 * L, D))
    x1_dram = dint("x1_dram", (L, D))
    x2_dram = dint("x2_dram", (L, D))
    bcrows_d = dint("bcrows_d", (32, TC))
    wsc_d = dint("wsc_d", (D, 1))
    q_hi_d = dint("q_hi_d", (HEADS * DH, L), BF16)
    q_lo_d = dint("q_lo_d", (HEADS * DH, L), BF16)
    k_hi_d = dint("k_hi_d", (HEADS * DH, L), BF16)
    k_lo_d = dint("k_lo_d", (HEADS * DH, L), BF16)
    vt_hi_d = dint("vt_hi_d", (HEADS * L, DH), BF16)
    vt_lo_d = dint("vt_lo_d", (HEADS * L, DH), BF16)

    import contextlib
    with contextlib.ExitStack() as _es:
        def _e(cm):
            return _es.enter_context(cm)
        xtok = _e(nc.sbuf_tensor("xtok", [128, 4096], F32))
        bigA = _e(nc.sbuf_tensor("bigA", [128, NDT * TC], F32))
        bigC = _e(nc.sbuf_tensor("bigC", [128, NDT * TC], F32))
        bigD = _e(nc.sbuf_tensor("bigD", [128, NDT * TC], F32))
        bigE = _e(nc.sbuf_tensor("bigE", [128, NB * TC], F32))
        bigF = _e(nc.sbuf_tensor("bigF", [128, NB * TC], F32))
        bigG = _e(nc.sbuf_tensor("bigG", [128, 2 * NB * TC], F32))
        bf_a = _e(nc.sbuf_tensor("bf_a", [128, 4096], BF16))
        bf_b = _e(nc.sbuf_tensor("bf_b", [128, 4096], BF16))
        bf_c = _e(nc.sbuf_tensor("bf_c", [128, 2048], BF16))
        bf_d = _e(nc.sbuf_tensor("bf_d", [128, 2048], BF16))
        gbx = _e(nc.sbuf_tensor("gbx", [128, 2 * D], F32))
        dump = _e(nc.sbuf_tensor("dump", [128, D], F32))
        xi_a = _e(nc.sbuf_tensor("xi_a", [128, TC + 8], F32))
        xi_b = _e(nc.sbuf_tensor("xi_b", [128, TC + 8], F32))
        tails = _e(nc.sbuf_tensor("tails", [128, NDT * 3], F32))
        zsT = _e(nc.sbuf_tensor("zsT", [128, TC], F32))
        sm_a = _e(nc.sbuf_tensor("sm_a", [128, TC], F32))
        sm_b = _e(nc.sbuf_tensor("sm_b", [128, TC], F32))
        gev = _e(nc.sbuf_tensor("gev", [128, 2 * TC], F32))
        gi_hi = _e(nc.sbuf_tensor("gi_hi", [128, TC], BF16))
        gi_lo = _e(nc.sbuf_tensor("gi_lo", [128, TC], BF16))
        stat = _e(nc.sbuf_tensor("stat", [128, 48], F32))
        carry = _e(nc.sbuf_tensor("carry", [128, DS * NDT], F32))
        Esb = _e(nc.sbuf_tensor("Esb", [128, NDT * DS], F32))
        cvw = _e(nc.sbuf_tensor("cvw", [128, NDT * DC], F32))
        cvb = _e(nc.sbuf_tensor("cvb", [128, NDT], F32))
        dtbn = _e(nc.sbuf_tensor("dtbn", [128, NDT], F32))
        Dcol = _e(nc.sbuf_tensor("Dcol", [128, NDT], F32))
        wqkv = _e(nc.sbuf_tensor("wqkv", [128, 6 * DH], BF16))
        ones1 = _e(nc.sbuf_tensor("ones1", [1, 128], F32))
        nones1 = _e(nc.sbuf_tensor("nones1", [1, 128], F32))
        eyesb = _e(nc.sbuf_tensor("eyesb", [128, 128], F32))
        eyeJsb = _e(nc.sbuf_tensor("eyeJsb", [128, 128], F32))
        tr_sb = _e(nc.sbuf_tensor("tr_sb", [128, 128], F32))
        sclr = _e(nc.sbuf_tensor("sclr", [128, 8], F32))
        ibwd = _e(nc.sbuf_tensor("ibwd", [1, 1], I32))
        ps0 = _e(nc.psum_tensor("ps0", [128, TC], F32))
        ps1 = _e(nc.psum_tensor("ps1", [128, TC], F32))
        ps2 = _e(nc.psum_tensor("ps2", [128, TC], F32))
        ps3 = _e(nc.psum_tensor("ps3", [128, TC], F32))
        ps4 = _e(nc.psum_tensor("ps4", [128, TC], F32))
        ps5 = _e(nc.psum_tensor("ps5", [128, TC], F32))
        ps6 = _e(nc.psum_tensor("ps6", [128, TC], F32))
        ps7 = _e(nc.psum_tensor("ps7", [128, TC], F32))
        _SEMNAMES = ["sMM", "sSC", "sVE", "sGP",
                     "sLCON", "sLX", "sLC", "sLD2", "sLA", "sLB", "sLIA", "sLIB",
                     "sLG", "sLDU", "sLQ0", "sLQ1", "sLK0", "sLK1", "sLV0", "sLV1",
                     "sLM", "sLBIG", "sLQB", "sTP0", "sTP1", "sTN",
                     "sDW", "sDS", "sAC0", "sAC1", "sPW", "sDX", "sDQ", "sDO"]
        sems = {n: _e(nc.semaphore(n)) for n in _SEMNAMES}
        block = _e(nc.Block())
        a_rot = xtok[:, 0:NB * TC]
        xdbl_dt = zsT[0:64, 0:TC]
        bc_sb = zsT
        wsb = gbx[:, D:2 * D]
        avq = bigE
        xpw = bigF[:, 0:NDT * 96]
        dtw = bigE[0:64, 0:DI]
        _FV = bigF.bitcast(BF16)
        ksb_h = _FV[:, 0:1024]
        ksb_l = _FV[:, 1024:2048]
        _GV8 = bigG.bitcast(F8)
        _GVB = bigG.bitcast(BF16)
        kq_sb = _GV8
        biasb = _GVB
        vsb_h = _GVB[:, 5120:5632]
        vsb_l = _GVB[:, 5632:6144]
        ptb = _e(nc.sbuf_tensor("ptb_t", [128, 512], BF16))[:]
        msk = bigG[:, 3328:3584]
        xq8 = _GV8[:, 14336:15360]
        PS = [ps0, ps1, ps2, ps3, ps4, ps5, ps6, ps7]
        C = {k: 0 for k in sems}

        class _Cap:
            def __init__(self, eng):
                self._eng = eng
                self.last = None

            def __getattr__(self, n):
                f = getattr(self._eng, n)
                if callable(f):
                    def w(*a, **k):
                        r = f(*a, **k)
                        if hasattr(r, "then_inc"):
                            self.last = r
                        return r
                    return w
                return f

        def run(engname, fn, waits=(), inc=None, incn=1):
            dec = getattr(block, engname)
            res = [None]

            @dec
            def _(eng):
                for s, v in waits:
                    if v > 0:
                        eng.wait_ge(sems[s], v)
                cap = _Cap(eng)
                r = fn(cap)
                if inc is not None:
                    C[inc] += incn
                    tgt = r if r is not None else cap.last
                    if tgt is not None:
                        tgt.then_inc(sems[inc], incn)
                    else:
                        eng.maybe_drain_then_inc((sems[inc], incn))
                    res[0] = C[inc]
            return res[0]

        def dma(eng, out, in_, sem, accum=False):
            kw = {"accum_op": ALU.add} if accum else {}
            eng.dma_start(out=out, in_=in_, **kw).then_inc(sems[sem], 16)
            C[sem] += 16

        def fn(sy):
            for a, b_ in ((eyesb[:], eye[:]), (eyeJsb[:], eyeJ[:]),
                          (cvb[:, :], conv_bc[:]), (dtbn[:, :], dt_bc[:]),
                          (Dcol[:, :], Dvecc[:]), (cvw[:], conv_wc[:]),
                          (Esb[:], A_logc[:]), (gbx[0:1, 0:D], gammav[:]),
                          (gbx[0:1, D:2 * D], betav[:]), (ibwd[:], is_bwd[:])):
                dma(sy, a, b_, "sLCON")
        run("sync", fn)
        ld_consts = C["sLCON"]

        def fn(v):
            v.memset(ones1[:], 1.0)
            v.memset(nones1[:], -1.0)
            v.memset(carry[:], 0.0)
            v.memset(tails[:], 0.0)
            v.tensor_scalar_mul(dtbn[:], dtbn[:], -1.0)
        ve_neg = run("vector", fn, [("sLCON", ld_consts)], inc="sVE")

        def fn(s):
            return s.activation(out=Esb[:], in_=Esb[:], func=ACT.Exp)
        sc_E = run("scalar", fn, [("sVE", ve_neg)], inc="sSC")

        def fn(t):
            t.matmul(ps0[:, 0:TC], ones1[:], gbx[0:1, 0:TC], start=True, stop=True)
            t.matmul(ps1[:, 0:TC], ones1[:], gbx[0:1, TC:2 * TC], start=True, stop=True)
            t.matmul(ps2[:, 0:TC], ones1[:], gbx[0:1, D:D + TC], start=True, stop=True)
            t.matmul(ps3[:, 0:TC], ones1[:], gbx[0:1, D + TC:D + 2 * TC], start=True, stop=True)
        mm_gb = run("tensor", fn, [("sLCON", ld_consts), ("sVE", ve_neg)], inc="sMM")

        def fn(v):
            v.tensor_copy(out=gbx[:, 0:TC], in_=ps0[:, 0:TC])
            v.tensor_copy(out=gbx[:, TC:2 * TC], in_=ps1[:, 0:TC])
            v.tensor_copy(out=gbx[:, D:D + TC], in_=ps2[:, 0:TC])
            v.tensor_copy(out=gbx[:, D + TC:2 * D], in_=ps3[:, 0:TC])
        ve_gb = run("vector", fn, [("sMM", mm_gb)], inc="sVE")

        def split_weight(src, dst_hi, dst_lo, rows, cols):
            for c0 in range(0, cols, 4096):
                cw = min(4096, cols - c0)
                for i in range(rows // 128):
                    def fl(sy, i=i, c0=c0, cw=cw):
                        dma(sy, xtok[:, 0:cw], src[i * 128:(i + 1) * 128, c0:c0 + cw], "sLX")
                    run("sync", fl, [("sDW", C["sDW"])])
                    ldv = C["sLX"]

                    def fv(v, cw=cw):
                        v.tensor_copy(out=bf_a[:, 0:cw], in_=xtok[:, 0:cw])
                        return v.tensor_tensor(out=bf_b[:, 0:cw], in0=xtok[:, 0:cw],
                                               in1=bf_a[:, 0:cw], op=ALU.subtract)
                    ve = run("vector", fv, [("sLX", ldv)], inc="sVE")

                    def fg(g, i=i, c0=c0, cw=cw):
                        dma(g, dst_hi[i * 128:(i + 1) * 128, c0:c0 + cw], bf_a[:, 0:cw], "sDW")
                        dma(g, dst_lo[i * 128:(i + 1) * 128, c0:c0 + cw], bf_b[:, 0:cw], "sDW")
                    run("gpsimd", fg, [("sVE", ve)])

        split_weight(in_wT, inw_hi, inw_lo, D, 2 * DI)
        split_weight(mout_wT, mout_hi, mout_lo, DI, D)
        split_weight(outw_halfT, outw_hi, outw_lo, D, D)
        split_weight(WoT, wo_hi, wo_lo, D, D)

        for i in range(NKD):
            def fl(sy, i=i):
                dma(sy, xtok[:, 0:D], ffn_w[i * 128:(i + 1) * 128, :], "sLX")
            run("sync", fl, [("sDW", C["sDW"]), ("sTN", C["sTN"])])
            ldv = C["sLX"]

            def fv(v, i=i):
                v.tensor_reduce(stat[:, 40:41], xtok[:, 0:D], AX.X, ALU.add,
                                apply_absolute_value=True)
                v.drain()
                v.tensor_scalar(out=stat[:, 40:41], in0=stat[:, 40:41],
                                scalar1=1.0 / D, scalar2=1e-5, op0=ALU.mult, op1=ALU.max)
                v.drain()
                v.reciprocal(out=stat[:, 41:42], in_=stat[:, 40:41])
                v.drain()
                v.tensor_copy(out=stat[:, 42:43], in_=stat[:, 40:41])
                v.drain()
                v.tensor_scalar(out=dump[:, 0:D], in0=xtok[:, 0:D],
                                scalar1=stat[:, 41:42], scalar2=None, op0=ALU.mult)
                v.tensor_single_scalar(out=xtok[:, 0:D], in_=dump[:, 0:D],
                                       scalar=0.5, op=ALU.is_gt)
                v.tensor_single_scalar(out=dump[:, 0:D], in_=dump[:, 0:D],
                                       scalar=-0.5, op=ALU.is_lt)
                return v.tensor_tensor(out=bf_a[:, 0:D], in0=xtok[:, 0:D],
                                       in1=dump[:, 0:D], op=ALU.subtract)
            ve = run("vector", fv, [("sLX", ldv)], inc="sVE")

            def fs(s, i=i):
                for j in range(NKD):
                    s.dma_start_transpose(out=bf_b[:, j * 128:(j + 1) * 128],
                                          in_=bf_a[:, j * 128:(j + 1) * 128]
                                          ).then_inc(sems["sTN"], 16)
                C["sTN"] += NKD * 16
            run("scalar", fs, [("sVE", ve)])
            trv = C["sTN"]

            def fg(g, i=i):
                for j in range(NKD):
                    dma(g, ternT_d[j * 128:(j + 1) * 128, i * 128:(i + 1) * 128],
                        bf_b[:, j * 128:(j + 1) * 128], "sDW")
                dma(g, wsc_d[i * 128:(i + 1) * 128, :], stat[:, 42:43], "sDW")
            run("gpsimd", fg, [("sTN", trv)])


        def fn(sy):
            dma(sy, sclr[0:1, 0:1], s_inv[:], "sLCON")
            dma(sy, sclr[0:1, 1:2], s_kv[:], "sLCON")
        run("sync", fn, [])
        ld_s = C["sLCON"]

        def fn(t):
            t.matmul(ps0[:, 0:2], ones1[:], sclr[0:1, 0:2], start=True, stop=True)
        mm_s = run("tensor", fn, [("sLCON", ld_s), ("sVE", ve_gb), ("sSC", C["sSC"])], inc="sMM")

        def fn(v):
            v.tensor_copy(out=sclr[:, 0:2], in_=ps0[:, 0:2])
            v.drain()
            v.reciprocal(out=sclr[:, 2:3], in_=sclr[:, 0:1])
            v.reciprocal(out=sclr[:, 3:4], in_=sclr[:, 1:2])
            v.drain()
            v.tensor_tensor(out=sclr[:, 4:5], in0=sclr[:, 0:1],
                            in1=sclr[:, 1:2], op=ALU.mult)
            v.drain()
        ve_s = run("vector", fn, [("sMM", mm_s)], inc="sVE")

        def fn(sy):
            dma(sy, dump[0:DH, 0:DH], WqT[:], "sLCON")
            dma(sy, dump[0:DH, DH:2 * DH], WkT[:], "sLCON")
            dma(sy, dump[0:DH, 2 * DH:3 * DH], WvT[:], "sLCON")
        run("sync", fn, [("sVE", C["sVE"]), ("sSC", C["sSC"])])
        ld_w = C["sLCON"]

        def fn(v):
            v.tensor_copy(out=wqkv[0:DH, 0:3 * DH], in_=dump[0:DH, 0:3 * DH])
            return v.tensor_tensor(out=wqkv[0:DH, 3 * DH:6 * DH],
                                   in0=dump[0:DH, 0:3 * DH],
                                   in1=wqkv[0:DH, 0:3 * DH], op=ALU.subtract)
        ve_wq = run("vector", fn, [("sLCON", ld_w)], inc="sVE")

        def fn(g):
            dma(g, wqkv[64:128, 0:6 * DH], wqkv[0:64, 0:6 * DH], "sLQB")
        run("gpsimd", fn, [("sVE", ve_wq)])
        wq2_dm = C["sLQB"]

        prep_dw = C["sDW"]
        prep_ve = C["sVE"]
        prep_sc = C["sSC"]

        prev = dict(gat_last=0, acc_floor=0, bcrow_mm=0, bcdm=0)
        for k in range(NCH):
            def fn(sy, k=k):
                for i in range(4):
                    r0 = k * TC + i * 128
                    dma(sy, xtok[:, i * D:(i + 1) * D], x_l0[r0:r0 + 128, :], "sLX")
                dma(sy, xpw[:].rearrange("p (d c) -> p d c", d=NDT),
                    xproj_wT[:].rearrange("(d p) c -> p d c", p=128), "sLX")
                dma(sy, dtw[:, :], dt_wT[:, :], "sLX")
            run("sync", fn, [("sMM", C["sMM"]), ("sSC", C["sSC"]), ("sVE", C["sVE"]),
                             ("sDS", C["sDS"]), ("sAC0", C["sAC0"]), ("sAC1", C["sAC1"]),
                             ("sPW", C["sPW"]), ("sDW", C["sDW"])])
            ld_x = C["sLX"]

            def fn(v):
                for i in range(4):
                    v.tensor_reduce(stat[:, i:i + 1], xtok[:, i * D:(i + 1) * D],
                                    AX.X, ALU.add)
            ve_sum = run("vector", fn, [("sLX", ld_x)], inc="sVE")

            def fn(s):
                for i in range(4):
                    s.activation(out=dump[:], in_=xtok[:, i * D:(i + 1) * D],
                                 func=ACT.Square, accum_out=stat[:, 4 + i:5 + i])
            sc_sq = run("scalar", fn, [("sLX", ld_x), ("sSC", prep_sc if k == 0 else 0)],
                        inc="sSC")

            def fn(v):
                v.tensor_scalar_mul(stat[:, 8:12], stat[:, 0:4], 1.0 / D)
                v.tensor_scalar_mul(stat[:, 12:16], stat[:, 4:8], 1.0 / D)
                v.drain()
                v.tensor_tensor(out=stat[:, 16:20], in0=stat[:, 8:12],
                                in1=stat[:, 8:12], op=ALU.mult)
                v.drain()
                v.tensor_tensor(out=stat[:, 12:16], in0=stat[:, 12:16],
                                in1=stat[:, 16:20], op=ALU.subtract)
                v.drain()
                v.tensor_scalar_add(stat[:, 16:20], stat[:, 12:16], 1e-5)
            ve_st = run("vector", fn, [("sVE", ve_sum), ("sSC", sc_sq)], inc="sVE")

            def fn(s):
                s.activation(out=stat[:, 20:24], in_=stat[:, 16:20], func=ACT.Ln)
                s.drain()
                s.activation(out=stat[:, 20:24], in_=stat[:, 20:24], func=ACT.Exp,
                             scale=-0.5)
            sc_r1 = run("scalar", fn, [("sVE", ve_st)], inc="sSC")

            def fn(v):
                v.tensor_tensor(out=stat[:, 24:28], in0=stat[:, 20:24],
                                in1=stat[:, 20:24], op=ALU.mult)
                v.drain()
                v.tensor_tensor(out=stat[:, 24:28], in0=stat[:, 24:28],
                                in1=stat[:, 12:16], op=ALU.mult)
                v.drain()
                v.tensor_scalar_add(stat[:, 24:28], stat[:, 24:28], 1e-5)
            ve_v2 = run("vector", fn, [("sSC", sc_r1)], inc="sVE")

            def fn(s):
                s.activation(out=stat[:, 28:32], in_=stat[:, 24:28], func=ACT.Ln)
                s.drain()
                s.activation(out=stat[:, 28:32], in_=stat[:, 28:32], func=ACT.Exp,
                             scale=-0.5)
            sc_r2 = run("scalar", fn, [("sVE", ve_v2)], inc="sSC")

            def fn(v):
                v.tensor_tensor(out=stat[:, 20:24], in0=stat[:, 20:24],
                                in1=stat[:, 28:32], op=ALU.mult)
                v.drain()
                v.tensor_tensor(out=stat[:, 32:36], in0=stat[:, 8:12],
                                in1=stat[:, 20:24], op=ALU.mult)
                v.drain()
                for i in range(4):
                    xa = xtok[:, i * D:(i + 1) * D]
                    v.tensor_scalar(out=xa, in0=xa, scalar1=stat[:, 20 + i:21 + i],
                                    scalar2=stat[:, 32 + i:33 + i],
                                    op0=ALU.mult, op1=ALU.subtract)
                    v.tensor_tensor(out=xa, in0=xa, in1=gbx[:, 0:D], op=ALU.mult)
                    v.tensor_tensor(out=xa, in0=xa, in1=gbx[:, D:2 * D], op=ALU.add)
            ve_h = run("vector", fn,
                       [("sSC", sc_r2), ("sVE", ve_gb if k == 0 else 0)], inc="sVE")

            tr_sc = {}
            tr_ve = {}
            for n in range(32):
                i, j = divmod(n, NKD)
                psT = PS[6] if n % 2 == 0 else PS[7]

                def fn(t, i=i, j=j, psT=psT):
                    return t.transpose(psT[:, 0:128],
                                       xtok[:, i * D + j * 128:i * D + (j + 1) * 128],
                                       eyesb[:])
                wv = [("sVE", ve_h)]
                if n >= 2:
                    wv.append(("sVE", tr_ve[n - 2]))
                mm_n = run("tensor", fn, wv, inc="sMM")

                def fn(s, i=i, j=j, psT=psT):
                    return s.copy(out=bf_a[:, j * TC + i * 128:j * TC + (i + 1) * 128],
                                  in_=psT[:, 0:128])
                tr_sc[n] = run("scalar", fn, [("sMM", mm_n)], inc="sSC")

                def fn(v, i=i, j=j, psT=psT):
                    sl = slice(j * TC + i * 128, j * TC + (i + 1) * 128)
                    return v.tensor_tensor(out=bf_b[:, sl], in0=psT[:, 0:128],
                                           in1=bf_a[:, sl], op=ALU.subtract)
                tr_ve[n] = run("vector", fn, [("sSC", tr_sc[n])], inc="sVE")
            ve_hsplit = C["sVE"]

            def stream_w_og(sy, og, buf):
                sem = "sLC" if buf is bf_c else "sLD2"
                for kt in range(NKD):
                    dma(sy, buf[:, (2 * kt) * 128:(2 * kt + 1) * 128],
                        inw_hi[kt * 128:(kt + 1) * 128, og * 128:(og + 1) * 128], sem)
                    dma(sy, buf[:, (2 * kt + 1) * 128:(2 * kt + 2) * 128],
                        inw_lo[kt * 128:(kt + 1) * 128, og * 128:(og + 1) * 128], sem)
                return sem

            ip_ld = {}
            ip_mm = {}
            ip_ev = {}
            cv_ve = {}
            zs_dm = {}

            def fn(sy):
                stream_w_og(sy, 0, bf_c)
                ip_ld[0] = ("sLC", C["sLC"])
                stream_w_og(sy, 1, bf_d)
                ip_ld[1] = ("sLD2", C["sLD2"])
            run("sync", fn, [("sDW", prep_dw if k == 0 else 0)])

            for og in range(32):
                buf = bf_c if og % 2 == 0 else bf_d
                psm = PS[og % 2]

                def fn(t, og=og, buf=buf, psm=psm):
                    for kt in range(NKD):
                        whi = buf[:, (2 * kt) * 128:(2 * kt + 1) * 128]
                        wlo = buf[:, (2 * kt + 1) * 128:(2 * kt + 2) * 128]
                        xhi = bf_a[:, kt * TC:(kt + 1) * TC]
                        xlo = bf_b[:, kt * TC:(kt + 1) * TC]
                        t.matmul(psm[:], whi, xhi, start=(kt == 0), stop=False)
                        t.matmul(psm[:], whi, xlo, start=False, stop=False)
                        t.matmul(psm[:], wlo, xhi, start=False, stop=(kt == NKD - 1))
                wv = [ip_ld[og], ("sVE", ve_hsplit)]
                if og >= 2:
                    wv.append(("sSC", ip_ev[og - 2]))
                ip_mm[og] = run("tensor", fn, wv, inc="sMM")

                def fn(sy, og=og, buf=buf):
                    if og + 2 < 32:
                        sem = stream_w_og(sy, og + 2, buf)
                        ip_ld[og + 2] = (sem, C[sem])
                run("sync", fn, [("sMM", ip_mm[og])])

                if og < NDT:
                    xib = xi_a if og % 2 == 0 else xi_b

                    def fn(s, og=og, psm=psm, xib=xib):
                        return s.copy(out=xib[:, 3:3 + TC], in_=psm[:])
                    wv = [("sMM", ip_mm[og])]
                    if og >= 2:
                        wv.append(("sVE", cv_ve[og - 2]))
                    ip_ev[og] = run("scalar", fn, wv, inc="sSC")

                    def fn(v, og=og, xib=xib):
                        v.tensor_copy(out=xib[:, 0:3], in_=tails[:, og * 3:og * 3 + 3])
                        v.tensor_copy(out=tails[:, og * 3:og * 3 + 3],
                                      in_=xib[:, TC:TC + 3])
                        v.tensor_scalar(out=gev[:, 0:TC], in0=xib[:, 0:TC],
                                        scalar1=cvw[:, og * DC:og * DC + 1],
                                        scalar2=cvb[:, og:og + 1],
                                        op0=ALU.mult, op1=ALU.add)
                        for kk in range(1, DC):
                            v.scalar_tensor_tensor(
                                out=gev[:, 0:TC], in0=xib[:, kk:kk + TC],
                                scalar=cvw[:, og * DC + kk:og * DC + kk + 1],
                                in1=gev[:, 0:TC], op0=ALU.mult, op1=ALU.add)
                    wv = [("sSC", ip_ev[og])]
                    if og >= 2:
                        wv.append(("sDS", zs_dm.get(og - 2, 0)))
                    ve_cv = run("vector", fn, wv, inc="sVE")

                    def fn(s, og=og):
                        return s.activation(out=gev[:, TC:2 * TC], in_=gev[:, 0:TC],
                                            func=ACT.Silu)
                    sc_xc = run("scalar", fn, [("sVE", ve_cv)], inc="sSC")

                    def fn(t, og=og):
                        t.matmul(PS[2][0:64, 0:TC], xpw[:, og * 96:og * 96 + 64],
                                 gev[:, TC:2 * TC], start=(og == 0), stop=(og == NDT - 1))
                        t.matmul(PS[3][0:32, 0:TC], xpw[:, og * 96 + 64:og * 96 + 96],
                                 gev[:, TC:2 * TC], start=(og == 0), stop=(og == NDT - 1))
                    mm_xp = run("tensor", fn, [("sSC", sc_xc)], inc="sMM")

                    def fn(g, og=og, k=k):
                        r0 = (k * NDT + og) * 128
                        dma(g, xc_dram[r0:r0 + 128, :], gev[:, TC:2 * TC], "sDS")
                    run("gpsimd", fn, [("sMM", mm_xp)])
                    zs_dm[og] = C["sDS"]
                    cv_ve[og] = run("vector", lambda v: None,
                                    [("sDS", C["sDS"])], inc="sVE")
                else:
                    def fn(s, og=og, psm=psm):
                        return s.activation(out=zsT[:], in_=psm[:], func=ACT.Silu)
                    wv = [("sMM", ip_mm[og])]
                    if og - 1 in zs_dm and og > NDT:
                        wv.append(("sDS", zs_dm[og - 1]))
                    ip_ev[og] = run("scalar", fn, wv, inc="sSC")

                    def fn(g, og=og, k=k):
                        r0 = (k * NDT + (og - NDT)) * 128
                        dma(g, zs_dram[r0:r0 + 128, :], zsT[:], "sDS")
                    run("gpsimd", fn, [("sSC", ip_ev[og])])
                    zs_dm[og] = C["sDS"]

            mm_xproj = C["sMM"]

            def fn(s):
                s.copy(out=xdbl_dt[:, :], in_=PS[2][0:64, 0:TC])
                return s.copy(out=bc_sb[64:96, :], in_=PS[3][0:32, 0:TC])
            sc_xd = run("scalar", fn,
                        [("sMM", mm_xproj), ("sDS", C["sDS"])], inc="sSC")

            def fn(g):
                dma(g, bcrows_d[:, :], bc_sb[64:96, :], "sDS")
            run("gpsimd", fn, [("sSC", sc_xd)])
            dm_bc = C["sDS"]
            prev["bcdm"] = dm_bc

            dt_mm = {}
            dt_ev = {}
            for og in range(NDT):
                psm = PS[4] if og % 2 == 0 else PS[5]

                def fn(t, og=og, psm=psm):
                    return t.matmul(psm[:], dtw[:, og * 128:(og + 1) * 128],
                                    xdbl_dt[:, :], start=True, stop=True)
                wv = [("sSC", sc_xd)]
                if og >= 2:
                    wv.append(("sSC", dt_ev[og - 2]))
                dt_mm[og] = run("tensor", fn, wv, inc="sMM")

                def fn(s, og=og, psm=psm):
                    s.activation(out=dump[:, 0:TC], in_=psm[:], func=ACT.Sigmoid,
                                 scale=-1.0, bias=dtbn[:, og:og + 1])
                    return s.activation(out=bigC[:, og * TC:(og + 1) * TC],
                                        in_=dump[:, 0:TC], func=ACT.Ln)
                dt_ev[og] = run("scalar", fn, [("sMM", dt_mm[og])], inc="sSC")
            sc_dneg = C["sSC"]

            dun_ve = {}
            xc_ld = {}
            for og in range(NDT):
                sem = "sLA" if og % 2 == 0 else "sLB"

                def fn(sy, og=og, k=k, sem=sem):
                    r0 = (k * NDT + og) * 128
                    dma(sy, (sm_a if og % 2 == 0 else sm_b)[:, :],
                        xc_dram[r0:r0 + 128, :], sem)
                wv = [("sDS", C["sDS"])] if og < 2 else [("sVE", dun_ve[og - 2])]
                run("sync", fn, wv)
                xc_ld[og] = C[sem]

                def fn(v, og=og):
                    return v.tensor_tensor(out=bigD[:, og * TC:(og + 1) * TC],
                                           in0=bigC[:, og * TC:(og + 1) * TC],
                                           in1=(sm_a if og % 2 == 0 else sm_b)[:, :],
                                           op=ALU.mult)
                dun_ve[og] = run("vector", fn,
                                 [(sem, xc_ld[og]), ("sSC", dt_ev[og])], inc="sVE")
            ve_dun = C["sVE"]

            scan_done = {}
            acc_dm = {}
            tmp_last = {}
            bc_ld = {}
            for s2 in range(DS):
                bsem = "sLG" if s2 % 2 == 0 else "sLDU"

                def fn(sy, s2=s2, bsem=bsem):
                    row = gev if s2 % 2 == 0 else dump
                    dma(sy, row[0:1, 0:TC], bcrows_d[s2:s2 + 1, :], bsem)
                    dma(sy, row[0:1, TC:2 * TC], bcrows_d[16 + s2:17 + s2, :], bsem)
                wv = [("sDS", dm_bc), ("sSC", sc_dneg)]
                if s2 >= 2:
                    wv.append(("sMM", bc_ld[s2 - 2]))
                run("sync", fn, wv)
                ld_bc = C[bsem]

                def fn(t, s2=s2):
                    row = gev if s2 % 2 == 0 else dump
                    t.matmul(PS[6][:, 0:TC], nones1[:],
                             row[0:1, 0:TC], start=True, stop=True)
                    t.matmul(PS[7][:, 0:TC], ones1[:],
                             row[0:1, TC:2 * TC], start=True, stop=True)
                wv = [(bsem, ld_bc)]
                if s2 > 0:
                    wv.append(("sVE", tmp_last[s2 - 1]))
                run("tensor", fn, wv, inc="sMM")
                mm_bc = C["sMM"]
                bc_ld[s2] = mm_bc

                for qb in range(NDT // NB):
                    dts = list(range(qb * NB, (qb + 1) * NB))

                    def fn(s, s2=s2, dts=dts):
                        for j, dt_i in enumerate(dts):
                            s.activation(out=a_rot[:, j * TC:(j + 1) * TC],
                                         in_=bigC[:, dt_i * TC:(dt_i + 1) * TC],
                                         func=ACT.Exp,
                                         scale=Esb[:, dt_i * DS + s2:dt_i * DS + s2 + 1])
                    wv = [("sSC", sc_dneg)]
                    pkey = (s2, qb - 1) if qb > 0 else ((s2 - 1, 3) if s2 > 0 else None)
                    if pkey in scan_done:
                        wv.append(("sVE", scan_done[pkey]))
                    sc_a = run("scalar", fn, wv, inc="sSC")

                    def fn(v, s2=s2, qb=qb):
                        src = bigD[:, qb * NB * TC:(qb + 1) * NB * TC]
                        return v.tensor_tensor(
                            out=bigE[:].rearrange("p (j t) -> p j t", j=NB),
                            in0=src.rearrange("p (j t) -> p j t", j=NB),
                            in1=PS[6][:, 0:TC].unsqueeze(1).broadcast_to([128, NB, TC]),
                            op=ALU.mult)
                    wv = [("sMM", mm_bc), ("sVE", ve_dun)]
                    if pkey in scan_done:
                        wv.append(("sVE", scan_done[pkey]))
                    ve_g = run("vector", fn, wv, inc="sVE")

                    def fn(v, s2=s2, qb=qb, dts=dts):
                        for j, dt_i in enumerate(dts):
                            v.tensor_tensor_scan(
                                out=bigF[:, j * TC:(j + 1) * TC],
                                data0=a_rot[:, j * TC:(j + 1) * TC],
                                data1=bigE[:, j * TC:(j + 1) * TC],
                                initial=carry[:, s2 * NDT + dt_i:s2 * NDT + dt_i + 1],
                                op0=ALU.mult, op1=ALU.add)
                    ve_scan = run("vector", fn, [("sSC", sc_a)], inc="sVE")

                    half = (s2 * 4 + qb) % 2

                    def fn(v, s2=s2, qb=qb, half=half):
                        dst = bigG[:, half * NB * TC:(half + 1) * NB * TC]
                        v.tensor_tensor(
                            out=dst.rearrange("p (j t) -> p j t", j=NB),
                            in0=bigF[:].rearrange("p (j t) -> p j t", j=NB),
                            in1=PS[7][:, 0:TC].unsqueeze(1).broadcast_to([128, NB, TC]),
                            op=ALU.mult)
                        return v.tensor_copy(
                            out=carry[:, s2 * NDT + qb * NB:s2 * NDT + (qb + 1) * NB],
                            in_=bigF[:, TC - 1::TC])
                    wv = [("sVE", ve_scan)]
                    k2 = (s2, qb - 2) if qb >= 2 else ((s2 - 1, qb + 2) if s2 > 0 else None)
                    if k2 in acc_dm:
                        wv.append(acc_dm[k2])
                    ve_tmp = run("vector", fn, wv, inc="sVE")
                    scan_done[(s2, qb)] = ve_tmp
                    tmp_last[s2] = ve_tmp

                    asem = "sAC0" if half == 0 else "sAC1"

                    def fn(g, s2=s2, qb=qb, half=half, asem=asem):
                        srcb = bigG[:, half * NB * TC:(half + 1) * NB * TC]
                        dma(g, bigA[:, qb * NB * TC:(qb + 1) * NB * TC], srcb, asem,
                            accum=(s2 > 0))
                    wv = [("sVE", ve_tmp)]
                    if s2 == 0:
                        wv.append(("sVE", prev["gat_last"]))
                    run("gpsimd", fn, wv)
                    acc_dm[(s2, qb)] = (asem, C[asem])
            scan_acc = [("sAC0", C["sAC0"]), ("sAC1", C["sAC1"])]

            gt_ld = {}
            mo_mm = {}
            gat_ve = {}
            for dt_i in range(NDT):
                sems3 = ("sLA", "sLIA", "sLC") if dt_i % 2 == 0 else ("sLB", "sLIB", "sLD2")

                def fn(sy, dt_i=dt_i, k=k, sems3=sems3):
                    r0 = (k * NDT + dt_i) * 128
                    dma(sy, (sm_a if dt_i % 2 == 0 else sm_b)[:, :],
                        xc_dram[r0:r0 + 128, :], sems3[0])
                    dma(sy, (xi_a if dt_i % 2 == 0 else xi_b)[:, 0:TC],
                        zs_dram[r0:r0 + 128, :], sems3[1])
                    buf = bf_c if dt_i % 2 == 0 else bf_d
                    for og in range(8):
                        dma(sy, buf[:, (2 * og) * 128:(2 * og + 1) * 128],
                            mout_hi[dt_i * 128:(dt_i + 1) * 128, og * 128:(og + 1) * 128],
                            sems3[2])
                        dma(sy, buf[:, (2 * og + 1) * 128:(2 * og + 2) * 128],
                            mout_lo[dt_i * 128:(dt_i + 1) * 128, og * 128:(og + 1) * 128],
                            sems3[2])
                wv = (scan_acc + [("sVE", dun_ve[dt_i])]) if dt_i < 2 else                      [("sMM", mo_mm[dt_i - 2])]
                run("sync", fn, wv)
                gt_ld[dt_i] = [(s, C[s]) for s in sems3]

                def fn(v, dt_i=dt_i):
                    xc_t = (sm_a if dt_i % 2 == 0 else sm_b)[:, :]
                    zs_t = (xi_a if dt_i % 2 == 0 else xi_b)[:, 0:TC]
                    v.scalar_tensor_tensor(out=gev[:, 0:TC], in0=xc_t,
                                           scalar=Dcol[:, dt_i:dt_i + 1],
                                           in1=bigA[:, dt_i * TC:(dt_i + 1) * TC],
                                           op0=ALU.mult, op1=ALU.add)
                    v.tensor_tensor(out=gev[:, TC:2 * TC], in0=gev[:, 0:TC],
                                    in1=zs_t, op=ALU.mult)
                    v.tensor_copy(out=gi_hi[:], in_=gev[:, TC:2 * TC])
                    return v.tensor_tensor(out=gi_lo[:], in0=gev[:, TC:2 * TC],
                                           in1=gi_hi[:], op=ALU.subtract)
                wv = gt_ld[dt_i] + scan_acc
                if dt_i >= 2:
                    wv.append(("sMM", mo_mm[dt_i - 2]))
                gat_ve[dt_i] = run("vector", fn, wv, inc="sVE")

                def fn(t, dt_i=dt_i):
                    buf = bf_c if dt_i % 2 == 0 else bf_d
                    for og in range(8):
                        whi = buf[:, (2 * og) * 128:(2 * og + 1) * 128]
                        wlo = buf[:, (2 * og + 1) * 128:(2 * og + 2) * 128]
                        t.matmul(PS[og][:, 0:TC], whi, gi_hi[:],
                                 start=(dt_i == 0), stop=False)
                        t.matmul(PS[og][:, 0:TC], whi, gi_lo[:], start=False, stop=False)
                        t.matmul(PS[og][:, 0:TC], wlo, gi_hi[:], start=False,
                                 stop=(dt_i == NDT - 1))
                mo_mm[dt_i] = run("tensor", fn, [("sVE", gat_ve[dt_i])], inc="sMM")
            prev["gat_last"] = gat_ve[NDT - 1]
            mm_mo = C["sMM"]

            def fn(s):
                for og in range(8):
                    s.copy(out=bigC[:, og * TC:(og + 1) * TC], in_=PS[og][:, 0:TC])
            sc_mo = run("scalar", fn, [("sMM", mm_mo)], inc="sSC")

            def fn(v):
                v.tensor_copy(out=bf_a[:, 0:4096], in_=bigC[:, 0:8 * TC])
                return v.tensor_tensor(out=bf_b[:, 0:4096], in0=bigC[:, 0:8 * TC],
                                       in1=bf_a[:, 0:4096], op=ALU.subtract)
            ve_mo = run("vector", fn, [("sSC", sc_mo)], inc="sVE")

            op_mm = {}
            op_ld = {}
            op_ev = {}
            for og in range(8):
                buf = bf_c if og % 2 == 0 else bf_d

                osem = "sLC" if og % 2 == 0 else "sLD2"

                def fn(sy, og=og, buf=buf, osem=osem):
                    for kt in range(8):
                        dma(sy, buf[:, (2 * kt) * 128:(2 * kt + 1) * 128],
                            outw_hi[kt * 128:(kt + 1) * 128, og * 128:(og + 1) * 128], osem)
                        dma(sy, buf[:, (2 * kt + 1) * 128:(2 * kt + 2) * 128],
                            outw_lo[kt * 128:(kt + 1) * 128, og * 128:(og + 1) * 128], osem)
                wv = [("sMM", mm_mo)] if og < 2 else [("sMM", op_mm[og - 2])]
                run("sync", fn, wv)
                op_ld[og] = (osem, C[osem])

                psm = PS[og % 2]

                def fn(t, og=og, buf=buf, psm=psm):
                    for kt in range(8):
                        whi = buf[:, (2 * kt) * 128:(2 * kt + 1) * 128]
                        wlo = buf[:, (2 * kt + 1) * 128:(2 * kt + 2) * 128]
                        xhi = bf_a[:, kt * TC:(kt + 1) * TC]
                        xlo = bf_b[:, kt * TC:(kt + 1) * TC]
                        t.matmul(psm[:, 0:TC], whi, xhi, start=(kt == 0), stop=False)
                        t.matmul(psm[:, 0:TC], whi, xlo, start=False, stop=False)
                        t.matmul(psm[:, 0:TC], wlo, xhi, start=False, stop=(kt == 7))
                wv = [op_ld[og], ("sVE", ve_mo)]
                if og >= 2:
                    wv.append(("sSC", op_ev[og - 2]))
                op_mm[og] = run("tensor", fn, wv, inc="sMM")

                def fn(s, og=og, psm=psm):
                    return s.copy(out=bigD[:, og * TC:(og + 1) * TC], in_=psm[:, 0:TC])
                op_ev[og] = run("scalar", fn, [("sMM", op_mm[og])], inc="sSC")
            sc_po = C["sSC"]

            po_ev = {}
            for n in range(32):
                og, i = divmod(n, 4)
                psT = PS[6] if n % 2 == 0 else PS[7]

                def fn(t, og=og, i=i, psT=psT):
                    return t.transpose(psT[:, 0:128],
                                       bigD[:, og * TC + i * 128:og * TC + (i + 1) * 128],
                                       eyesb[:])
                wv = [("sSC", sc_po)]
                if n >= 2:
                    wv.append(("sPW", po_ev[n - 2]))
                mm_n = run("tensor", fn, wv, inc="sMM")

                def fn(s, psT=psT):
                    return s.copy(out=tr_sb[:], in_=psT[:, 0:128])
                wv2 = [("sMM", mm_n)]
                if n >= 1:
                    wv2.append(("sPW", po_ev[n - 1]))
                sc_n = run("scalar", fn, wv2, inc="sSC")

                def fn(g, og=og, i=i, k=k):
                    r0 = k * TC + i * 128
                    dma(g, p_ag_in[r0:r0 + 128, og * 128:(og + 1) * 128], tr_sb[:], "sPW")
                run("gpsimd", fn, [("sSC", sc_n)])
                po_ev[n] = C["sPW"]
        l0_dm = C["sPW"]

        if stage == "l0dbg":
            def fn(g):
                dma(g, dbg[0:128, :], bigA[:, 0:NDT * TC], "sDO")
                dma(g, dbg[128:256, :], bigC[:, 0:NDT * TC], "sDO")
                dma(g, dbg[256:384, :], bigD[:, 0:NDT * TC], "sDO")
            run("gpsimd", fn, [("sPW", l0_dm), ("sAC0", C["sAC0"]), ("sAC1", C["sAC1"]),
                               ("sMM", C["sMM"]), ("sSC", C["sSC"]), ("sVE", C["sVE"])])

            def fn(g):
                g.wait_ge(sems["sDO"], C["sDO"])
            run("gpsimd", fn, [])
            return nc

        if stage in ("xc", "pag"):
            def fn(g):
                if stage == "xc":
                    for i in range(NCH * NDT):
                        dma(g, dbg[i * 128:(i + 1) * 128, 0:TC],
                            xc_dram[i * 128:(i + 1) * 128, :], "sDO")
                        dma(g, dbg[i * 128:(i + 1) * 128, TC:2 * TC],
                            zs_dram[i * 128:(i + 1) * 128, :], "sDO")
                else:
                    for i in range(NTT):
                        dma(g, dbg[i * 128:(i + 1) * 128, :],
                            p_ag_in[i * 128:(i + 1) * 128, :], "sDO")
            run("gpsimd", fn, [("sPW", l0_dm), ("sDS", C["sDS"]),
                               ("sAC0", C["sAC0"]), ("sAC1", C["sAC1"]),
                               ("sMM", C["sMM"]), ("sSC", C["sSC"]), ("sVE", C["sVE"])])

            def fn(g):
                g.wait_ge(sems["sDO"], C["sDO"])
            run("gpsimd", fn, [])
            return nc

        if stage == "nocc":
            def fn(g):
                g.dma_start(out=p_ag_out[0:L, :], in_=p_ag_in[:, :]).then_inc(sems["sGP"], 16)
                g.dma_start(out=p_ag_out[L:2 * L, :], in_=p_ag_in[:, :]).then_inc(sems["sGP"], 16)
            run("gpsimd", fn, [("sPW", l0_dm)])
            C["sGP"] += 32
            cc1 = C["sGP"]
        else:
            def fn(g):
                return g.collective_compute(
                    "AllGather", ALU.bypass,
                    replica_groups=[[0, 1], [2, 3], [4, 5], [6, 7]],
                    ins=[p_ag_in.opt()], outs=[p_ag_out.opt()],
                )
            cc1 = run("gpsimd", fn, [("sPW", l0_dm)], inc="sGP")

        def fn(sy):
            dma(sy, gev[0:1, 0:D], wsc_d[:, :].transpose([1, 0]), "sLG")
            dma(sy, bf_b[0:1, 0:D], l1biasv[0:1, :], "sLQ1")
        run("sync", fn, [("sGP", cc1), ("sMM", C["sMM"]), ("sVE", C["sVE"]),
                         ("sDW", C["sDW"])])
        ld_pe = [("sLG", C["sLG"]), ("sLQ1", C["sLQ1"])]

        def fn(v):
            return v.tensor_copy(out=dump[0:1, 0:D], in_=bf_b[0:1, 0:D])
        ve_pe = run("vector", fn, ld_pe, inc="sVE")

        def fn(t):
            t.matmul(ps0[:, 0:TC], ones1[:], gev[0:1, 0:TC], start=True, stop=True)
            t.matmul(ps1[:, 0:TC], ones1[:], gev[0:1, TC:2 * TC], start=True, stop=True)
            t.matmul(ps2[:, 0:TC], ones1[:], dump[0:1, 0:TC], start=True, stop=True)
            t.matmul(ps3[:, 0:TC], ones1[:], dump[0:1, TC:2 * TC],
                     start=True, stop=True)
        mm_pe = run("tensor", fn, [("sVE", ve_pe)], inc="sMM")

        def fn(v):
            v.tensor_copy(out=wsb[:, 0:TC], in_=ps0[:, 0:TC])
            v.tensor_copy(out=wsb[:, TC:2 * TC], in_=ps1[:, 0:TC])
            v.tensor_copy(out=biasb[:, 4096:4096 + TC], in_=ps2[:, 0:TC])
            v.tensor_copy(out=biasb[:, 4096 + TC:4096 + 2 * TC], in_=ps3[:, 0:TC])
        ve_pe2 = run("vector", fn, [("sMM", mm_pe)], inc="sVE")

        kq_ve = {}
        for i in range(NKD):
            def fl(sy, i=i):
                dma(sy, bf_a[:, 0:D], l1k[i * 128:(i + 1) * 128, :], "sLQ0")
            run("sync", fl, [("sVE", kq_ve.get(i - 1, ve_pe2))])
            ldv = C["sLQ0"]

            def fv(v, i=i):
                return v.tensor_scalar(out=kq_sb[:, i * D:(i + 1) * D],
                                       in0=bf_a[:, 0:D], scalar1=sclr[:, 3:4],
                                       scalar2=None, op0=ALU.mult)
            kq_ve[i] = run("vector", fv, [("sLQ0", ldv)], inc="sVE")

        hadm_sb = bigC.bitcast(BF16)
        tern_sb = bigD.bitcast(BF16)

        def fn(sy):
            dma(sy, gbx[0:1, 0:D], out_bv[:], "sLBIG")
            for i in range(NKD):
                dma(sy, hadm_sb[:, i * D:(i + 1) * D], hadm[i * 128:(i + 1) * 128, :], "sLBIG")
                dma(sy, tern_sb[:, i * D:(i + 1) * D], ternT_d[i * 128:(i + 1) * 128, :], "sLBIG")
        run("sync", fn, [("sGP", cc1), ("sVE", C["sVE"]), ("sDW", C["sDW"])])
        ld_p3 = C["sLBIG"]

        def fn(t):
            t.matmul(ps0[:, 0:TC], ones1[:], gbx[0:1, 0:TC], start=True, stop=True)
            t.matmul(ps1[:, 0:TC], ones1[:], gbx[0:1, TC:2 * TC], start=True, stop=True)
        mm_ob = run("tensor", fn, [("sLBIG", ld_p3), ("sGP", cc1)], inc="sMM")

        def fn(v):
            v.tensor_copy(out=gbx[:, 0:TC], in_=ps0[:, 0:TC])
            v.tensor_copy(out=gbx[:, TC:2 * TC], in_=ps1[:, 0:TC])
        ve_ob = run("vector", fn, [("sMM", mm_ob)], inc="sVE")

        p3_dm = {}
        p3_ve = {}
        p3_tr = {}
        for tt in range(NTT):
            def fn(sy, tt=tt):
                dma(sy, xtok[:, 0:D], x_nat[tt * 128:(tt + 1) * 128, :], "sLX")
                dma(sy, xtok[:, D:2 * D], p_ag_out[tt * 128:(tt + 1) * 128, :], "sLX")
                dma(sy, xtok[:, 2 * D:3 * D],
                    p_ag_out[L + (NTT - 1 - tt) * 128:L + (NTT - tt) * 128, :], "sLX")
            wv = [("sLBIG", ld_p3), ("sVE", ve_ob), ("sMM", C["sMM"]),
                  ("sSC", C["sSC"])] if tt == 0 else \
                 [("sDX", p3_dm[tt - 1]), ("sVE", p3_ve[tt - 1]), ("sTN", C["sTN"])]
            run("sync", fn, wv)
            ld_t = C["sLX"]

            def fn(t, tt=tt):
                t.matmul(ps6[:, 0:TC], eyeJsb[:], xtok[:, 2 * D:2 * D + TC],
                         start=True, stop=True)
                t.matmul(ps7[:, 0:TC], eyeJsb[:], xtok[:, 2 * D + TC:3 * D],
                         start=True, stop=True)
            wv = [("sLX", ld_t)]
            if tt > 0:
                wv.append(("sVE", p3_ve[tt - 1]))
            mm_j = run("tensor", fn, wv, inc="sMM")

            def fn(v):
                v.tensor_reduce(stat[:, 0:1], xtok[:, 0:D], AX.X, ALU.add)
            ve_s1 = run("vector", fn, [("sLX", ld_t)], inc="sVE")

            def fn(s):
                s.activation(out=dump[:], in_=xtok[:, 0:D], func=ACT.Square,
                             accum_out=stat[:, 1:2])
            sc_s1 = run("scalar", fn, [("sLX", ld_t), ("sSC", C["sSC"])], inc="sSC")

            def fn(v):
                v.tensor_scalar_mul(stat[:, 2:3], stat[:, 0:1], 1.0 / D)
                v.tensor_scalar_mul(stat[:, 3:4], stat[:, 1:2], 1.0 / D)
                v.drain()
                v.tensor_tensor(out=stat[:, 4:5], in0=stat[:, 2:3],
                                in1=stat[:, 2:3], op=ALU.mult)
                v.drain()
                v.tensor_tensor(out=stat[:, 3:4], in0=stat[:, 3:4],
                                in1=stat[:, 4:5], op=ALU.subtract)
                v.drain()
                v.tensor_scalar_add(stat[:, 4:5], stat[:, 3:4], 1e-5)
            ve_s2 = run("vector", fn, [("sVE", ve_s1), ("sSC", sc_s1)], inc="sVE")

            def fn(s):
                s.activation(out=stat[:, 5:6], in_=stat[:, 4:5], func=ACT.Ln)
                s.drain()
                s.activation(out=stat[:, 5:6], in_=stat[:, 5:6], func=ACT.Exp,
                             scale=-0.5)
            sc_r = run("scalar", fn, [("sVE", ve_s2)], inc="sSC")

            def fn(v):
                v.tensor_tensor(out=stat[:, 6:7], in0=stat[:, 2:3],
                                in1=stat[:, 5:6], op=ALU.mult)
                v.drain()
                v.tensor_scalar(out=xtok[:, 3 * D:4 * D], in0=xtok[:, 0:D],
                                scalar1=stat[:, 5:6], scalar2=stat[:, 6:7],
                                op0=ALU.mult, op1=ALU.subtract)
                v.tensor_tensor(out=gev[:, 0:D], in0=xtok[:, 0:D],
                                in1=xtok[:, D:2 * D], op=ALU.add)
                v.tensor_tensor(out=gev[:, 0:D], in0=gev[:, 0:D],
                                in1=xtok[:, 3 * D:4 * D], op=ALU.add)
                v.tensor_tensor(out=gev[:, 0:D], in0=gev[:, 0:D],
                                in1=gbx[:, 0:D], op=ALU.add)
                v.tensor_tensor(out=gev[:, 0:TC], in0=gev[:, 0:TC],
                                in1=ps6[:, 0:TC], op=ALU.add)
                return v.tensor_tensor(out=gev[:, TC:2 * TC], in0=gev[:, TC:2 * TC],
                                       in1=ps7[:, 0:TC], op=ALU.add)
            ve_x1 = run("vector", fn, [("sSC", sc_r), ("sMM", mm_j)], inc="sVE")

            if dbg is not None and stage == "x1":
                def fn(g, tt=tt):
                    dma(g, dbg[tt * 128:(tt + 1) * 128, :], gev[:, 0:D], "sDX")
                run("gpsimd", fn, [("sVE", ve_x1)])
                p3_dm[tt] = C["sDX"]
                p3_ve[tt] = ve_x1
                continue

            def fn(v):
                v.tensor_reduce(stat[:, 8:9], gev[:, 0:D], AX.X, ALU.add)
            ve_f1 = run("vector", fn, [("sVE", ve_x1)], inc="sVE")

            def fn(s):
                s.activation(out=dump[:], in_=gev[:, 0:D], func=ACT.Square,
                             accum_out=stat[:, 9:10])
            sc_f1 = run("scalar", fn, [("sVE", ve_x1)], inc="sSC")

            def fn(v):
                v.tensor_scalar_mul(stat[:, 10:11], stat[:, 8:9], 1.0 / D)
                v.tensor_scalar_mul(stat[:, 11:12], stat[:, 9:10], 1.0 / D)
                v.drain()
                v.tensor_tensor(out=stat[:, 12:13], in0=stat[:, 10:11],
                                in1=stat[:, 10:11], op=ALU.mult)
                v.drain()
                v.tensor_tensor(out=stat[:, 11:12], in0=stat[:, 11:12],
                                in1=stat[:, 12:13], op=ALU.subtract)
                v.drain()
                v.tensor_scalar_add(stat[:, 12:13], stat[:, 11:12], 1e-5)
            ve_f2 = run("vector", fn, [("sVE", ve_f1), ("sSC", sc_f1)], inc="sVE")

            def fn(s):
                s.activation(out=stat[:, 13:14], in_=stat[:, 12:13], func=ACT.Ln)
                s.drain()
                s.activation(out=stat[:, 13:14], in_=stat[:, 13:14], func=ACT.Exp,
                             scale=-0.5)
            sc_f2 = run("scalar", fn, [("sVE", ve_f2)], inc="sSC")

            def fn(v):
                v.tensor_tensor(out=stat[:, 14:15], in0=stat[:, 10:11],
                                in1=stat[:, 13:14], op=ALU.mult)
                v.drain()
                return v.tensor_scalar(out=xtok[:, D:2 * D], in0=gev[:, 0:D],
                                       scalar1=stat[:, 13:14], scalar2=stat[:, 14:15],
                                       op0=ALU.mult, op1=ALU.subtract)
            ve_ln4 = run("vector", fn, [("sSC", sc_f2)], inc="sVE")

            trs = {}
            for j in range(NKD):
                psT = ps6 if j % 2 == 0 else ps7

                def fn(t, j=j, psT=psT):
                    return t.transpose(psT[:, 0:128],
                                       xtok[:, D + j * 128:D + (j + 1) * 128], eyesb[:])
                wv = [("sVE", ve_ln4)]
                if j >= 2:
                    wv.append(("sVE", trs[j - 2]))
                mm_t = run("tensor", fn, wv, inc="sMM")

                def fn(s, j=j, psT=psT):
                    return s.copy(out=bf_a[:, j * 128:(j + 1) * 128], in_=psT[:, 0:128])
                sc_t = run("scalar", fn, [("sMM", mm_t)], inc="sSC")

                def fn(v, j=j, psT=psT):
                    return v.tensor_tensor(out=bf_b[:, j * 128:(j + 1) * 128],
                                           in0=psT[:, 0:128],
                                           in1=bf_a[:, j * 128:(j + 1) * 128],
                                           op=ALU.subtract)
                trs[j] = run("vector", fn, [("sSC", sc_t)], inc="sVE")
            ve_lnT = C["sVE"]

            def fn(t):
                for kt in range(NKD):
                    lhi = bf_a[:, kt * 128:(kt + 1) * 128]
                    llo = bf_b[:, kt * 128:(kt + 1) * 128]
                    for hb, psm in ((0, ps0), (1, ps1)):
                        hm = hadm_sb[:, kt * D + hb * TC:kt * D + (hb + 1) * TC]
                        t.matmul(psm[:, 0:TC], lhi, hm, start=(kt == 0), stop=False)
                        t.matmul(psm[:, 0:TC], llo, hm, start=False,
                                 stop=(kt == NKD - 1))
            mm_h = run("tensor", fn, [("sVE", ve_lnT)], inc="sMM")

            def fn(v):
                v.tensor_reduce(stat[:, 16:17], ps0[:, 0:TC], AX.X, ALU.max,
                                apply_absolute_value=True)
                v.tensor_reduce(stat[:, 17:18], ps1[:, 0:TC], AX.X, ALU.max,
                                apply_absolute_value=True)
                v.drain()
                v.tensor_tensor(out=stat[:, 16:17], in0=stat[:, 16:17],
                                in1=stat[:, 17:18], op=ALU.max)
                v.drain()
                v.tensor_scalar(out=stat[:, 16:17], in0=stat[:, 16:17],
                                scalar1=1e-5, scalar2=1.0 / 7.0,
                                op0=ALU.max, op1=ALU.mult)
                v.drain()
                v.reciprocal(out=stat[:, 18:19], in_=stat[:, 16:17])
                v.drain()
                v.tensor_scalar(out=xtok[:, 2 * D:2 * D + TC], in0=ps0[:, 0:TC],
                                scalar1=stat[:, 18:19], scalar2=None, op0=ALU.mult)
                v.tensor_scalar(out=xtok[:, 2 * D + TC:3 * D], in0=ps1[:, 0:TC],
                                scalar1=stat[:, 18:19], scalar2=None, op0=ALU.mult)
                v.tensor_copy(out=xtok.bitcast(I32)[:, 3 * D:4 * D],
                              in_=xtok[:, 2 * D:3 * D])
                v.tensor_copy(out=xtok[:, 2 * D:3 * D],
                              in_=xtok.bitcast(I32)[:, 3 * D:4 * D])
                return v.tensor_scalar(out=bf_a[:, D:2 * D], in0=xtok[:, 2 * D:3 * D],
                                       scalar1=7.0, scalar2=-8.0,
                                       op0=ALU.min, op1=ALU.max)
            ve_q = run("vector", fn, [("sMM", mm_h)], inc="sVE")

            def fn(s):
                for j in range(NKD):
                    s.dma_start_transpose(
                        out=bf_b[:, D + j * 128:D + (j + 1) * 128],
                        in_=bf_a[:, D + j * 128:D + (j + 1) * 128]
                        ).then_inc(sems["sTN"], 16)
                C["sTN"] += NKD * 16
            run("scalar", fn, [("sVE", ve_q)])
            tr_n = C["sTN"]

            def fn(t):
                for kt in range(NKD):
                    nT = bf_b[:, D + kt * 128:D + (kt + 1) * 128]
                    t.matmul(ps2[:, 0:TC], nT, tern_sb[:, kt * D:kt * D + TC],
                             start=(kt == 0), stop=(kt == NKD - 1))
                    t.matmul(ps3[:, 0:TC], nT, tern_sb[:, kt * D + TC:(kt + 1) * D],
                             start=(kt == 0), stop=(kt == NKD - 1))
            mm_i = run("tensor", fn, [("sTN", tr_n)], inc="sMM")

            def fn(s):
                s.activation(out=dump[:, 0:TC], in_=ps2[:, 0:TC], func=ACT.Copy,
                             scale=stat[:, 16:17])
                s.activation(out=dump[:, TC:2 * TC], in_=ps3[:, 0:TC], func=ACT.Copy,
                             scale=stat[:, 16:17])
            sc_y = run("scalar", fn, [("sMM", mm_i)], inc="sSC")

            def fn(v, tt=tt):
                v.tensor_tensor(out=dump[:, 0:D], in0=dump[:, 0:D],
                                in1=wsb[:, 0:D], op=ALU.mult)
                return v.tensor_tensor(out=xtok[:, 0:D], in0=gev[:, 0:D],
                                       in1=dump[:, 0:D], op=ALU.add)
            ve_x2 = run("vector", fn, [("sSC", sc_y)], inc="sVE")
            p3_ve[tt] = ve_x2

            def fn(g, tt=tt):
                dma(g, x2_dram[tt * 128:(tt + 1) * 128, :], xtok[:, 0:D], "sDX")
                if dbg is not None and stage == "x2":
                    dma(g, dbg[tt * 128:(tt + 1) * 128, :], xtok[:, 0:D], "sDX")
            run("gpsimd", fn, [("sVE", ve_x2)])
            p3_dm[tt] = C["sDX"]
        p3_done_dm = C["sDX"]

        if stage in ("x1", "x2"):
            def fn(g):
                g.wait_ge(sems["sDX"], C["sDX"])
            run("gpsimd", fn, [])
            return nc

        p4_dm = 0
        for kc in range(NCH):
            def fn(sy, kc=kc):
                for i in range(4):
                    r0 = kc * TC + i * 128
                    dma(sy, xtok[:, i * D:(i + 1) * D], x2_dram[r0:r0 + 128, :], "sLX")
            wv = [("sDX", p3_done_dm), ("sMM", C["sMM"]), ("sSC", C["sSC"]),
                  ("sVE", C["sVE"])]
            run("sync", fn, wv)
            ld_x2 = C["sLX"]

            def fn(v):
                for i in range(4):
                    v.tensor_reduce(stat[:, i:i + 1], xtok[:, i * D:(i + 1) * D],
                                    AX.X, ALU.add)
            ve_a = run("vector", fn, [("sLX", ld_x2)], inc="sVE")

            def fn(s):
                for i in range(4):
                    s.activation(out=dump[:], in_=xtok[:, i * D:(i + 1) * D],
                                 func=ACT.Square, accum_out=stat[:, 4 + i:5 + i])
            sc_a = run("scalar", fn, [("sLX", ld_x2)], inc="sSC")

            def fn(v):
                v.tensor_scalar_mul(stat[:, 8:12], stat[:, 0:4], 1.0 / D)
                v.tensor_scalar_mul(stat[:, 12:16], stat[:, 4:8], 1.0 / D)
                v.drain()
                v.tensor_tensor(out=stat[:, 16:20], in0=stat[:, 8:12],
                                in1=stat[:, 8:12], op=ALU.mult)
                v.drain()
                v.tensor_tensor(out=stat[:, 12:16], in0=stat[:, 12:16],
                                in1=stat[:, 16:20], op=ALU.subtract)
                v.drain()
                v.tensor_scalar_add(stat[:, 16:20], stat[:, 12:16], 1e-5)
            ve_b = run("vector", fn, [("sVE", ve_a), ("sSC", sc_a)], inc="sVE")

            def fn(s):
                s.activation(out=stat[:, 20:24], in_=stat[:, 16:20], func=ACT.Ln)
                s.drain()
                s.activation(out=stat[:, 20:24], in_=stat[:, 20:24], func=ACT.Exp,
                             scale=-0.5)
            sc_b = run("scalar", fn, [("sVE", ve_b)], inc="sSC")

            def fn(v):
                v.tensor_tensor(out=stat[:, 24:28], in0=stat[:, 8:12],
                                in1=stat[:, 20:24], op=ALU.mult)
                v.drain()
                for i in range(4):
                    xa = xtok[:, i * D:(i + 1) * D]
                    v.tensor_scalar(out=xa, in0=xa, scalar1=stat[:, 20 + i:21 + i],
                                    scalar2=stat[:, 24 + i:25 + i],
                                    op0=ALU.mult, op1=ALU.subtract)
            ve_ln3 = run("vector", fn, [("sSC", sc_b)], inc="sVE")

            trq = {}
            for n in range(32):
                i, j = divmod(n, NKD)
                psT = ps6 if n % 2 == 0 else ps7

                def fn(t, i=i, j=j, psT=psT):
                    return t.transpose(psT[:, 0:128],
                                       xtok[:, i * D + j * 128:i * D + (j + 1) * 128],
                                       eyesb[:])
                wv = [("sVE", ve_ln3)]
                if n >= 2:
                    wv.append(("sVE", trq[n - 2]))
                mm_t = run("tensor", fn, wv, inc="sMM")

                def fn(s, i=i, j=j, psT=psT):
                    return s.copy(out=bf_a[:, j * TC + i * 128:j * TC + (i + 1) * 128],
                                  in_=psT[:, 0:128])
                sc_t = run("scalar", fn, [("sMM", mm_t)], inc="sSC")

                def fn(v, i=i, j=j, psT=psT):
                    sl = slice(j * TC + i * 128, j * TC + (i + 1) * 128)
                    return v.tensor_tensor(out=bf_b[:, sl], in0=psT[:, 0:128],
                                           in1=bf_a[:, sl], op=ALU.subtract)
                trq[n] = run("vector", fn, [("sSC", sc_t)], inc="sVE")
            ve_l3T = C["sVE"]

            for h in range(HEADS):
                kt = h // 2
                pb = (h % 2) * 64
                rhs_hi = bf_a[pb:pb + 64, kt * TC:(kt + 1) * TC]
                rhs_lo = bf_b[pb:pb + 64, kt * TC:(kt + 1) * TC]
                for wi, (dhi, dlo) in enumerate(((q_hi_d, q_lo_d),
                                                 (k_hi_d, k_lo_d), (None, None))):
                    whi = wqkv[pb:pb + 64, wi * DH:(wi + 1) * DH]
                    wlo = wqkv[pb:pb + 64, (3 + wi) * DH:(4 + wi) * DH]
                    psm = ps4 if (h * 3 + wi) % 2 == 0 else ps5

                    def fn(t, whi=whi, wlo=wlo, psm=psm, rhs_hi=rhs_hi, rhs_lo=rhs_lo):
                        t.matmul(psm[0:64, 0:TC], whi, rhs_hi, start=True, stop=False)
                        t.matmul(psm[0:64, 0:TC], whi, rhs_lo, start=False, stop=False)
                        t.matmul(psm[0:64, 0:TC], wlo, rhs_hi, start=False, stop=True)
                    wv = [("sVE", ve_l3T), ("sLQB", wq2_dm)]
                    if True:
                        wv.append(("sSC", C["sSC"]))
                    mm_q = run("tensor", fn, wv, inc="sMM")

                    def fn(s, psm=psm):
                        return s.copy(out=dump[0:64, 0:TC], in_=psm[0:64, 0:TC])
                    sc_q = run("scalar", fn, [("sMM", mm_q)], inc="sSC")

                    def fn(v):
                        v.tensor_copy(out=bf_c[0:64, 0:TC], in_=dump[0:64, 0:TC])
                        return v.tensor_tensor(out=bf_d[0:64, 0:TC],
                                               in0=dump[0:64, 0:TC],
                                               in1=bf_c[0:64, 0:TC], op=ALU.subtract)
                    ve_q = run("vector", fn, [("sSC", sc_q), ("sDQ", C["sDQ"])], inc="sVE")

                    if wi < 2:
                        def fn(g, h=h, kc=kc, dhi=dhi, dlo=dlo):
                            c0 = kc * TC
                            dma(g, dhi[h * DH:(h + 1) * DH, c0:c0 + TC],
                                bf_c[0:64, 0:TC], "sDQ")
                            dma(g, dlo[h * DH:(h + 1) * DH, c0:c0 + TC],
                                bf_d[0:64, 0:TC], "sDQ")
                        run("gpsimd", fn, [("sVE", ve_q)])
                    else:
                        for i in range(4):
                            psT = ps6 if i % 2 == 0 else ps7

                            def fn(t, i=i, psT=psT):
                                return t.transpose(psT[:, 0:64],
                                                   dump[0:64, i * 128:(i + 1) * 128],
                                                   eyesb[0:64, 0:64])
                            mm_v = run("tensor", fn, [("sVE", ve_q)], inc="sMM")

                            def fn(s, psT=psT):
                                return s.copy(out=gev[:, 0:64], in_=psT[:, 0:64])
                            sc_v = run("scalar", fn, [("sMM", mm_v)], inc="sSC")

                            def fn(v):
                                v.tensor_copy(out=bf_c[:, TC:TC + 64], in_=gev[:, 0:64])
                                return v.tensor_tensor(out=bf_d[:, TC:TC + 64],
                                                       in0=gev[:, 0:64],
                                                       in1=bf_c[:, TC:TC + 64],
                                                       op=ALU.subtract)
                            ve_v = run("vector", fn, [("sSC", sc_v), ("sDQ", C["sDQ"])],
                                       inc="sVE")

                            def fn(g, h=h, kc=kc, i=i):
                                r0 = h * L + kc * TC + i * 128
                                dma(g, vt_hi_d[r0:r0 + 128, :], bf_c[:, TC:TC + 64], "sDQ")
                                dma(g, vt_lo_d[r0:r0 + 128, :], bf_d[:, TC:TC + 64], "sDQ")
                            run("gpsimd", fn, [("sVE", ve_v)])
            p4_dm = C["sDQ"]
        p4_done = C["sDQ"]

        rbwd = nc.gpsimd.alloc_register("rbwd")

        def fn(g):
            g.reg_load(rbwd, ibwd[0:1, 0:1])
        run("gpsimd", fn, [("sLCON", ld_consts)])

        out_dm = {}
        x4_ve = {}
        tmm_lastA = C["sMM"]
        mm_av_prev = C["sMM"]
        avq_val = {}
        for jj in range(OWN):
            kext = 2 * jj + 2
            kcols = kext * 128

            def fn(g, jj=jj):
                before = C["sLQB"]
                for arm in range(2):
                    ctxm = g.If_eq(rbwd, 0) if arm == 0 else g.Else()
                    with ctxm:
                        gidx = 2 * jj + arm
                        dma(g, bf_c[0:64, 0:2048].rearrange("p (h c) -> p h c", h=16),
                            q_hi_d[:, gidx * 128:(gidx + 1) * 128]
                            .rearrange("(h e) c -> e h c", e=64), "sLQB")
                        dma(g, bf_d[0:64, 0:2048].rearrange("p (h c) -> p h c", h=16),
                            q_lo_d[:, gidx * 128:(gidx + 1) * 128]
                            .rearrange("(h e) c -> e h c", e=64), "sLQB")
                        dma(g, gev[:, 0:D], x2_dram[gidx * 128:(gidx + 1) * 128, :], "sLQB")
                    if arm == 0:
                        C["sLQB"] = before
            wv = [("sDQ", p4_done)] if jj == 0 else [("sVE", x4_ve[jj - 1])]
            run("gpsimd", fn, wv)
            qx_dm = C["sLQB"]

            def fn(sy, jj=jj):
                dma(sy, msk[:, 0:256], maskd[jj * 128:(jj + 1) * 128, :], "sLM")
            run("sync", fn, [("sVE", C["sVE"] if jj > 0 else 0)])
            msk_ld = C["sLM"]

            for h in range(HEADS):
                hg = jj * HEADS + h
                njb = (kcols + TC - 1) // TC
                sc0 = C["sSC"]
                ve_gate = C["sVE"]
                mm_s = {}
                for jb in range(njb):
                    c0 = jb * TC
                    cw = min(TC, kcols - c0)
                    kb_h = ksb_h[:, (jb % 2) * TC:(jb % 2) * TC + TC]
                    kb_l = ksb_l[:, (jb % 2) * TC:(jb % 2) * TC + TC]
                    ksem = "sLK0" if jb % 2 == 0 else "sLK1"

                    def fn(sy, h=h, c0=c0, cw=cw, kb_h=kb_h, kb_l=kb_l, ksem=ksem):
                        dma(sy, kb_h[0:64, 0:cw], k_hi_d[h * DH:(h + 1) * DH, c0:c0 + cw], ksem)
                        dma(sy, kb_l[0:64, 0:cw], k_lo_d[h * DH:(h + 1) * DH, c0:c0 + cw], ksem)
                    wv = [("sLQB", qx_dm)]
                    if jb >= 2:
                        wv.append(("sSC", sc0 + jb - 1))
                    run("sync", fn, wv)
                    kld = C[ksem]

                    psm = ps0 if jb % 2 == 0 else ps1

                    def fn(t, h=h, cw=cw, psm=psm, kb_h=kb_h, kb_l=kb_l):
                        qhi = bf_c[0:64, h * 128:(h + 1) * 128]
                        qlo = bf_d[0:64, h * 128:(h + 1) * 128]
                        t.matmul(psm[:, 0:cw], qhi, kb_h[0:64, 0:cw], start=True, stop=False)
                        t.matmul(psm[:, 0:cw], qhi, kb_l[0:64, 0:cw], start=False, stop=False)
                        t.matmul(psm[:, 0:cw], qlo, kb_h[0:64, 0:cw], start=False, stop=True)
                    wv = [(ksem, kld), ("sLQB", qx_dm)]
                    if jb >= 2:
                        wv.append(("sSC", sc0 + jb - 1))
                    mm_s[jb] = run("tensor", fn, wv, inc="sMM")

                def fn(v, kcols=kcols):
                    v.tensor_tensor(out=xtok[:, kcols - 256:kcols - 128],
                                    in0=xtok[:, kcols - 256:kcols - 128],
                                    in1=msk[:, 0:128], op=ALU.add)
                    v.tensor_tensor(out=xtok[:, kcols - 128:kcols],
                                    in0=xtok[:, kcols - 128:kcols],
                                    in1=msk[:, 128:256], op=ALU.add)
                    return v.tensor_reduce(stat[:, 0:1], xtok[:, 0:kcols], AX.X,
                                           ALU.max, negate=True)
                ve_m = run("vector", fn, [("sSC", sc0 + njb), ("sLM", msk_ld)],
                           inc="sVE")

                mmb = C["sMM"]
                tmm_pred = {0: mmb + 1}
                _c = mmb + 1
                for _kt in range(kext):
                    if _kt + 1 < kext:
                        _c += 1
                        tmm_pred[_kt + 1] = _c
                    _c += 1
                av_last_pred = _c

                def fnsc(s, h=h, hg=hg, njb=njb, kext=kext, kcols=kcols,
                         ve_m=ve_m, mm_s=dict(mm_s), tA=tmm_lastA,
                         mavp=mm_av_prev, tmm_pred=dict(tmm_pred)):
                    for jb in range(njb):
                        c0 = jb * TC
                        cw = min(TC, kcols - c0)
                        psm = ps0 if jb % 2 == 0 else ps1
                        s.wait_ge(sems["sMM"], mm_s[jb])
                        s.activation(out=xtok[:, c0:c0 + cw], in_=psm[:, 0:cw],
                                     func=ACT.Copy, scale=0.125
                                     ).then_inc(sems["sSC"], 1)
                    s.wait_ge(sems["sVE"], ve_m)
                    s.wait_ge(sems["sMM"], tA)
                    s.activation(out=bigA[:, 0:kcols], in_=xtok[:, 0:kcols],
                                 func=ACT.Exp, bias=stat[:, 0:1],
                                 accum_out=stat[:, 1:2]).then_inc(sems["sSC"], 1)
                    if h > 0:
                        hp = h - 1
                        psAp = PS[4 + ((hp >> 1) & 1)]
                        pbp = (hp % 2) * 64
                        s.wait_ge(sems["sMM"], mavp)
                        s.copy(out=avq[pbp:pbp + 64, (hp // 2) * 128:(hp // 2 + 1) * 128],
                               in_=psAp[pbp:pbp + 64, 0:128]).then_inc(sems["sSC"], 1)
                    for kt in range(kext):
                        pst = (ps6, ps7, ps2, ps3)[kt % 4]
                        po = (kt % 2) * 256
                        s.wait_ge(sems["sMM"], tmm_pred[kt])
                        if kt >= 2:
                            s.wait_ge(sems["sVE"], ve_m + kt)
                        s.copy(out=ptb[:, po:po + 128],
                               in_=pst[:, 0:128]).then_inc(sems["sSC"], 1)
                wvs = [("sVE", ve_gate)] if h == 0 else []
                run("scalar", fnsc, wvs)
                C["sSC"] += njb + 1 + (1 if h > 0 else 0) + kext
                sc_e = sc0 + njb + 1
                if h > 0:
                    avq_val[hg - 1] = sc0 + njb + 2
                hi_base = sc0 + njb + 1 + (1 if h > 0 else 0)
                hi_val = {kt: hi_base + kt + 1 for kt in range(kext)}

                def fn(v, kcols=kcols):
                    v.reciprocal(out=stat[:, 2:3], in_=stat[:, 1:2])
                    v.drain()
                    return v.tensor_scalar(out=bigA[:, 0:kcols], in0=bigA[:, 0:kcols],
                                           scalar1=stat[:, 2:3], scalar2=None,
                                           op0=ALU.mult)
                ve_r = run("vector", fn, [("sSC", sc_e)], inc="sVE")

                if dbg is not None and stage == "x3" and jj == 0 and h == 0:
                    def fn(g):
                        dma(g, dbg3[:, 0:256], bigA[:, 0:256], "sDX")
                        dma(g, dbg3[:, 256:512], xtok[:, 0:256], "sDX")
                        dma(g, dbg3[:, 1024:2048], gev[:, 0:D], "sDX")
                    run("gpsimd", fn, [("sVE", ve_r)])

                psA = PS[4 + ((h >> 1) & 1)]
                pbv = (h % 2) * 64
                tmm = {}
                split_ve = {}
                av_mm = {}
                vld = {}

                def fn(t):
                    return t.transpose(ps6[:, 0:128], bigA[:, 0:128], eyesb[:])
                tmm[0] = run("tensor", fn, [("sVE", ve_r)], inc="sMM")

                for kt in range(kext):
                    jbv = kt // 4
                    vsem = "sLV0" if jbv % 2 == 0 else "sLV1"
                    if kt % 4 == 0:
                        def fn(sy, h=h, jbv=jbv, vsem=vsem, kcols=kcols):
                            r0 = h * L + jbv * TC
                            cw2 = min(TC, kcols - jbv * TC)
                            sl = (jbv % 2) * 256
                            dma(sy, vsb_h[:, sl:sl + (cw2 // 128) * 64]
                                .rearrange("p (k e) -> p k e", e=64),
                                vt_hi_d[r0:r0 + cw2, :]
                                .rearrange("(k p) e -> p k e", p=128), vsem)
                            dma(sy, vsb_l[:, sl:sl + (cw2 // 128) * 64]
                                .rearrange("p (k e) -> p k e", e=64),
                                vt_lo_d[r0:r0 + cw2, :]
                                .rearrange("(k p) e -> p k e", p=128), vsem)
                        wv = [("sVE", ve_r)]
                        if jbv >= 2:
                            wv.append(("sMM", av_mm[kt - 5]))
                        else:
                            wv.append(("sMM", mm_av_prev))
                        run("sync", fn, wv)
                        vld[jbv] = (vsem, C[vsem])

                    def fns(v, kt=kt):
                        pst = (ps6, ps7, ps2, ps3)[kt % 4]
                        po = (kt % 2) * 256
                        return v.tensor_tensor(out=ptb[:, po + 128:po + 256],
                                               in0=pst[:, 0:128],
                                               in1=ptb[:, po:po + 128],
                                               op=ALU.subtract)
                    split_ve[kt] = run("vector", fns,
                                       [("sSC", hi_val[kt]), ("sMM", tmm[kt])],
                                       inc="sVE")

                    def fnt(t, kt=kt, kext=kext, psA=psA, pbv=pbv,
                            spl=split_ve[kt],
                            spl_prev=(split_ve[kt - 3] if kt >= 3 else 0)):
                        if kt + 1 < kext:
                            if kt >= 3:
                                t.wait_ge(sems["sVE"], spl_prev)
                            pst2 = (ps6, ps7, ps2, ps3)[(kt + 1) % 4]
                            t.transpose(pst2[:, 0:128],
                                        bigA[:, (kt + 1) * 128:(kt + 2) * 128],
                                        eyesb[:]).then_inc(sems["sMM"], 1)
                        t.wait_ge(sems["sVE"], spl)
                        sl = (kt % 2) * 256
                        slv = ((kt // 4) % 2) * 256 + (kt % 4) * 64
                        vhi = vsb_h[:, slv:slv + 64]
                        vlo = vsb_l[:, slv:slv + 64]
                        phi = ptb[:, sl:sl + 128]
                        plo = ptb[:, sl + 128:sl + 256]
                        st = (kt == 0)
                        sp = (kt == kext - 1)
                        t.matmul(psA[pbv:pbv + 64, 0:128], vhi, phi, start=st, stop=False)
                        t.matmul(psA[pbv:pbv + 64, 0:128], vlo, phi, start=False, stop=False)
                        t.matmul(psA[pbv:pbv + 64, 0:128], vhi, plo, start=False, stop=sp)
                    wvv = [vld[jbv]]
                    if kt == 0 and hg >= 4:
                        wvv.append(("sSC", avq_val[hg - 4]))
                    if kt + 1 < kext:
                        C["sMM"] += 1
                        tmm[kt + 1] = C["sMM"]
                    av_mm[kt] = run("tensor", fnt, wvv, inc="sMM")
                assert tmm == tmm_pred, (jj, h, tmm, tmm_pred)
                assert av_mm[kext - 1] == av_last_pred
                mm_av_prev = av_mm[kext - 1]
                tmm_lastA = tmm[kext - 1]

            def fn(s, mavp=mm_av_prev):
                s.wait_ge(sems["sMM"], mavp)
                return s.copy(out=avq[64:128, 7 * 128:8 * 128],
                              in_=PS[5][64:128, 0:128])
            run("scalar", fn, [], inc="sSC")
            avq_val[jj * HEADS + 15] = C["sSC"]

            if dbg is not None and stage == "x3":
                def fn(g, jj=jj):
                    dma(g, dbg2[jj * 128:(jj + 1) * 128, :], avq[:, 0:1024], "sDX")
                run("gpsimd", fn, [("sSC", avq_val[jj * HEADS + 15])])

            def fn(v):
                v.tensor_copy(out=bf_c[:, 0:1024], in_=avq[:, 0:1024])
                return v.tensor_tensor(out=bf_d[:, 0:1024], in0=avq[:, 0:1024],
                                       in1=bf_c[:, 0:1024], op=ALU.subtract)
            ve_av = run("vector", fn, [("sSC", avq_val[jj * HEADS + 15])],
                        inc="sVE")

            wo_ld = {}
            wo_mm = {}
            for kt in range(8):
                wsem = "sLQ0" if kt % 2 == 0 else "sLQ1"

                def fn(sy, kt=kt, wsem=wsem):
                    sl = (kt % 2) * 2048
                    dma(sy, bf_a[:, sl:sl + 1024], wo_hi[kt * 128:(kt + 1) * 128, :], wsem)
                    dma(sy, bf_b[:, sl:sl + 1024], wo_lo[kt * 128:(kt + 1) * 128, :], wsem)
                wv = [("sVE", ve_av)]
                if kt >= 2:
                    wv.append(("sMM", wo_mm[kt - 2]))
                run("sync", fn, wv)
                wo_ld[kt] = (wsem, C[wsem])

                def fn(t, kt=kt):
                    sl = (kt % 2) * 2048
                    ahi = bf_c[:, kt * 128:(kt + 1) * 128]
                    alo = bf_d[:, kt * 128:(kt + 1) * 128]
                    st = (kt == 0)
                    sp = (kt == 7)
                    for hb, psm in ((0, ps2), (1, ps3)):
                        whi = bf_a[:, sl + hb * TC:sl + (hb + 1) * TC]
                        wlo = bf_b[:, sl + hb * TC:sl + (hb + 1) * TC]
                        t.matmul(psm[:, 0:TC], ahi, whi, start=st, stop=False)
                        t.matmul(psm[:, 0:TC], ahi, wlo, start=False, stop=False)
                        t.matmul(psm[:, 0:TC], alo, whi, start=False, stop=sp)
                wo_mm[kt] = run("tensor", fn, [wo_ld[kt], ("sVE", ve_av)],
                                inc="sMM")
            mm_wo = C["sMM"]

            def fn(v):
                v.tensor_tensor(out=xtok[:, 0:TC], in0=ps2[:, 0:TC],
                                in1=gev[:, 0:TC], op=ALU.add)
                return v.tensor_tensor(out=xtok[:, TC:2 * TC], in0=ps3[:, 0:TC],
                                       in1=gev[:, TC:2 * TC], op=ALU.add)
            ve_x3 = run("vector", fn, [("sMM", mm_wo)], inc="sVE")

            if dbg is not None and stage == "x3":
                def fn(g, jj=jj):
                    dma(g, dbg[jj * 128:(jj + 1) * 128, :], xtok[:, 0:D], "sDX")
                run("gpsimd", fn, [("sVE", ve_x3)])

            def fn(v):
                v.tensor_reduce(stat[:, 8:9], xtok[:, 0:D], AX.X, ALU.add)
            ve_l1 = run("vector", fn, [("sVE", ve_x3)], inc="sVE")

            def fn(s):
                s.activation(out=dump[:], in_=xtok[:, 0:D], func=ACT.Square,
                             accum_out=stat[:, 9:10])
            sc_l1 = run("scalar", fn, [("sVE", ve_x3)], inc="sSC")

            def fn(v):
                v.tensor_scalar_mul(stat[:, 10:11], stat[:, 8:9], 1.0 / D)
                v.tensor_scalar_mul(stat[:, 11:12], stat[:, 9:10], 1.0 / D)
                v.drain()
                v.tensor_tensor(out=stat[:, 12:13], in0=stat[:, 10:11],
                                in1=stat[:, 10:11], op=ALU.mult)
                v.drain()
                v.tensor_tensor(out=stat[:, 11:12], in0=stat[:, 11:12],
                                in1=stat[:, 12:13], op=ALU.subtract)
                v.drain()
                v.tensor_scalar_add(stat[:, 12:13], stat[:, 11:12], 1e-5)
            ve_l2 = run("vector", fn, [("sVE", ve_l1), ("sSC", sc_l1)], inc="sVE")

            def fn(s):
                s.activation(out=stat[:, 13:14], in_=stat[:, 12:13], func=ACT.Ln)
                s.drain()
                s.activation(out=stat[:, 13:14], in_=stat[:, 13:14], func=ACT.Exp,
                             scale=-0.5)
            sc_l2 = run("scalar", fn, [("sVE", ve_l2)], inc="sSC")

            def fn(v):
                v.tensor_tensor(out=stat[:, 14:15], in0=stat[:, 10:11],
                                in1=stat[:, 13:14], op=ALU.mult)
                v.drain()
                return v.tensor_scalar(out=bf_a[:, 0:1024], in0=xtok[:, 0:D],
                                       scalar1=stat[:, 13:14], scalar2=stat[:, 14:15],
                                       op0=ALU.mult, op1=ALU.subtract)
            ve_xb = run("vector", fn, [("sSC", sc_l2)], inc="sVE")

            def fn(s):
                for j in range(NKD):
                    s.dma_start_transpose(
                        out=bf_a[:, 1024 + j * 128:1024 + (j + 1) * 128],
                        in_=bf_a[:, j * 128:(j + 1) * 128]
                        ).then_inc(sems["sTN"], 16)
                C["sTN"] += NKD * 16
            run("scalar", fn, [("sVE", ve_xb)])
            tr_xb = C["sTN"]

            def fn(v):
                return v.tensor_scalar(out=xq8[:, 0:1024], in0=bf_a[:, 1024:2048],
                                       scalar1=sclr[:, 2:3], scalar2=None, op0=ALU.mult)
            ve_q8 = run("vector", fn, [("sTN", tr_xb)], inc="sVE")

            def fn(t):
                for kt2 in range(NKD):
                    xq = xq8[:, kt2 * 128:(kt2 + 1) * 128]
                    t.matmul(ps4[:, 0:TC], xq, kq_sb[:, kt2 * D:kt2 * D + TC],
                             start=(kt2 == 0), stop=(kt2 == NKD - 1))
                    t.matmul(ps5[:, 0:TC], xq, kq_sb[:, kt2 * D + TC:(kt2 + 1) * D],
                             start=(kt2 == 0), stop=(kt2 == NKD - 1))
            mm_f8 = run("tensor", fn, [("sVE", ve_q8)], inc="sMM")

            def fn(s):
                s.activation(out=bf_b[:, 0:512], in_=ps4[:, 0:TC], func=ACT.Copy,
                             scale=sclr[:, 4:5])
                return s.activation(out=bf_b[:, 512:1024], in_=ps5[:, 0:TC],
                                    func=ACT.Copy, scale=sclr[:, 4:5])
            sc_yb = run("scalar", fn, [("sMM", mm_f8)], inc="sSC")

            def fn(v, jj=jj):
                v.tensor_tensor(out=bf_b[:, 0:1024], in0=bf_b[:, 0:1024],
                                in1=biasb[:, 4096:4096 + D], op=ALU.add)
                return v.tensor_tensor(out=dump[:, 0:D], in0=xtok[:, 0:D],
                                       in1=bf_b[:, 0:1024], op=ALU.add)
            x4_ve[jj] = run("vector", fn, [("sSC", sc_yb)], inc="sVE")

            def fn(g, jj=jj):
                dma(g, out_own[jj * 128:(jj + 1) * 128, :], dump[:, 0:D], "sDO")
            run("gpsimd", fn, [("sVE", x4_ve[jj])])
            out_dm[jj] = C["sDO"]

        def fn(g):
            g.wait_ge(sems["sDO"], C["sDO"])
            g.wait_ge(sems["sDX"], C["sDX"])
        run("gpsimd", fn, [])

    return nc


def _prep_core_inputs(c, inputs):
    b, r = c // 2, c % 2
    tag = "f" if r == 0 else "b"
    g = lambda n: np.asarray(inputs[n])
    x = g("L0_attn_x")[b]
    m = {}
    m["x_l0"] = np.ascontiguousarray(x if r == 0 else x[::-1])
    m["x_nat"] = np.ascontiguousarray(x)
    m["in_wT"] = np.ascontiguousarray(g(f"L0_attn_in_w_{tag}").T)
    cw = g(f"L0_attn_conv_w_{tag}").reshape(DI, DC)
    m["conv_wc"] = np.ascontiguousarray(
        cw.reshape(NDT, 128, DC).transpose(1, 0, 2).reshape(128, NDT * DC))
    m["conv_bc"] = np.ascontiguousarray(
        g(f"L0_attn_conv_b_{tag}").reshape(NDT, 128).T)
    m["xproj_wT"] = np.ascontiguousarray(g(f"L0_attn_xproj_w_{tag}").T)
    m["dt_wT"] = np.ascontiguousarray(g(f"L0_attn_dt_w_{tag}").T)
    m["dt_bc"] = np.ascontiguousarray(g(f"L0_attn_dt_b_{tag}").reshape(NDT, 128).T)
    m["A_logc"] = np.ascontiguousarray(
        g(f"L0_attn_A_log_{tag}").reshape(NDT, 128, DS).transpose(1, 0, 2)
        .reshape(128, NDT * DS))
    m["Dvecc"] = np.ascontiguousarray(g(f"L0_attn_D_{tag}").reshape(NDT, 128).T)
    m["mout_wT"] = np.ascontiguousarray(g(f"L0_attn_mout_w_{tag}").T)
    m["outw_halfT"] = np.ascontiguousarray(
        g("L0_attn_out_w")[:, r * D:(r + 1) * D].T)
    m["out_bv"] = g("L0_attn_out_b").reshape(1, D)
    m["gammav"] = g("L0_attn_gamma").reshape(1, D)
    m["betav"] = g("L0_attn_beta").reshape(1, D)
    m["ffn_w"] = np.ascontiguousarray(g("L0_ffn_weight"))
    m["hadm"] = _hadamard(D).astype(ml_dtypes.bfloat16)
    m["WqT"] = np.ascontiguousarray(g("L1_attn_Wq").T)
    m["WkT"] = np.ascontiguousarray(g("L1_attn_Wk").T)
    m["WvT"] = np.ascontiguousarray(g("L1_attn_Wv").T)
    m["WoT"] = np.ascontiguousarray(g("L1_attn_Wo").T)
    m["l1k"] = np.ascontiguousarray(g("L1_ffn_kernel"))
    m["s_inv"] = g("L1_ffn_input_scale").reshape(1, 1)
    m["s_kv"] = g("L1_ffn_kernel_scale").reshape(1, 1)
    m["l1biasv"] = np.ascontiguousarray(g("L1_ffn_bias").reshape(1, D))
    m["eye128"] = np.eye(128, dtype=np.float32)
    m["eyeJ"] = np.eye(128, dtype=np.float32)[::-1].copy()
    mk = np.zeros((OWN * 128, 256), np.float32)
    for jj in range(OWN):
        gq = 2 * jj + r
        qg = gq * 128 + np.arange(128)[:, None]
        for e in range(2):
            e_kt = 2 * jj + e
            kg = e_kt * 128 + np.arange(128)[None, :]
            mk[jj * 128:(jj + 1) * 128, e * 128:(e + 1) * 128] = np.where(
                kg <= qg, 0.0, -1e30)
    m["maskd"] = mk
    m["is_bwd"] = np.array([[r]], dtype=np.int32)
    return m


_NC_CACHE = {}


def _get_nc(stage):
    if stage not in _NC_CACHE:
        _NC_CACHE[stage] = build(stage)
    return _NC_CACHE[stage]


def kernel(**inputs):
    nc = _get_nc("full")
    in_maps = [_prep_core_inputs(c, inputs) for c in range(8)]
    res = run_bass_kernel_spmd(nc, in_maps, core_ids=list(range(8)))
    out = np.empty((B, L, D), dtype=np.float32)
    for c in range(8):
        b, r = c // 2, c % 2
        oo = np.asarray(res.results[c]["out_own"])
        for jj in range(OWN):
            tt = 2 * jj + r
            out[b, tt * 128:(tt + 1) * 128, :] = oo[jj * 128:(jj + 1) * 128, :]
    return out
